# version 02 (final)
# baseline (speedup 1.0000x reference)
import numpy as np
import ml_dtypes
import concourse.bass as bass
import concourse.mybir as mybir
from concourse.bass_utils import run_bass_kernel_spmd

F32 = mybir.dt.float32
BF16 = mybir.dt.bfloat16
I32 = mybir.dt.int32
AF = mybir.ActivationFunctionType
ALU = mybir.AluOpType
AX = mybir.AxisListType
DS = bass.ds
NPBF = ml_dtypes.bfloat16

T, HID = 4096, 2048
FC = 16
CC = 8
C = 128
NCH = T // C
MID = 63
TT1, NT1 = 512, 8
TT3, NT3 = 512, 4
SCALE = 128 ** -0.5
IC_SH, IC_RT = 64, 8

_DEV = {"stop": None, "taps": ()}


class KB:

    def __init__(self, nc):
        self.nc = nc
        self._sems = {}
        self.cnt = {}
        self._lastwait = {}

    def sem(self, *names):
        for n in names:
            self._sems[n] = self.nc.alloc_semaphore(name=n)
            self.cnt[n] = 0

    def inc(self, inst, s, n=None):
        if n is None:
            n = 16 if s.startswith("q") else 1
        inst.then_inc(self._sems[s], n)
        self.cnt[s] += n
        return (s, self.cnt[s])

    def t(self, s):
        return (s, self.cnt[s])

    def dma(self, eng, sem, **kw):
        d = eng.dma_start(**kw)
        return self.inc(d, sem)

    def w(self, eng, *ticks):
        key = str(eng.engine)
        for s, v in ticks:
            if v <= 0:
                continue
            if v <= self._lastwait.get((key, s), 0):
                continue
            eng.wait_ge(self._sems[s], v)
            self._lastwait[(key, s)] = v


def _dram(nc, name, shape, dt, kind="Internal"):
    if kind == "Internal" and name in _DEV.get("taps", ()):
        kind = "ExternalOutput"
    return nc.dram_tensor(name, shape, dt, kind=kind).ap()



def emit_p1_projs(nc, kb, j, projs, xb, pp, ext, ybuf, st):
    for pi, P in enumerate(projs):
        for cc in range(CC):
            k = (pi * CC + cc) % 2
            kb.w(nc.tensor, st["pp_free"][k], st["xb_rdy"][j % 2],
                 st["w_rdy"])
            for fc in range(FC):
                ins = nc.tensor.matmul(
                    pp[k][:],
                    P["w"][:, fc * 1024 + cc * 128:fc * 1024 + (cc + 1) * 128],
                    xb[j % 2][:, fc * TT1:(fc + 1) * TT1],
                    start=(fc == 0), stop=(fc == FC - 1))
            mm_t = kb.inc(ins, "eP")
            st["xb_use"][j % 2] = mm_t
            eb = k
            slot = pi * CC + cc
            kb.w(nc.scalar, mm_t, st["ext_free"][eb],
                 st["carry_w"].get((P["name"], cc), ("eV", 0)))
            nc.scalar.copy(ext[eb][:, 0:3],
                           st["carry"][:, slot * 3:slot * 3 + 3])
            cin = nc.scalar.copy(ext[eb][:, 3:3 + TT1], pp[k][:])
            ext_t = kb.inc(cin, "eA")
            st["pp_free"][k] = ext_t
            yb = eb
            kb.w(nc.vector, ext_t, st["y_free"][yb])
            tap = P["taps"]
            nc.vector.tensor_scalar(
                ybuf[yb][:], ext[eb][:, 3:3 + TT1],
                tap[:, cc * 4 + 3:cc * 4 + 4], None, ALU.mult)
            for i in (2, 1, 0):
                nc.vector.scalar_tensor_tensor(
                    ybuf[yb][:], ext[eb][:, i:i + TT1],
                    tap[:, cc * 4 + i:cc * 4 + i + 1], ybuf[yb][:],
                    ALU.mult, ALU.add)
            csv = nc.vector.tensor_copy(
                st["carry"][:, slot * 3:slot * 3 + 3],
                ext[eb][:, TT1:TT1 + 3])
            y_t = kb.inc(csv, "eV")
            st["ext_free"][eb] = y_t
            st["carry_w"][(P["name"], cc)] = y_t
            sbn = j % 2
            kb.w(nc.scalar, y_t, st["stage_free"][(P["name"], sbn)])
            sil = nc.scalar.activation(
                P["stage"][sbn][:, cc * TT1:(cc + 1) * TT1], ybuf[yb][:],
                AF.Silu)
            sil_t = kb.inc(sil, "eA")
            st["y_free"][yb] = sil_t
        kb.w(nc.sync, sil_t)
        d = nc.sync.dma_start(
            out=P["scr"][:, j * TT1:(j + 1) * TT1]
            .rearrange("(cc p) t -> p cc t", p=128),
            in_=P["stage"][sbn][:].rearrange("p (cc t) -> p cc t", cc=CC))
        st["stage_free"][(P["name"], sbn)] = kb.inc(d, "qS")


def emit_p1_extras(nc, kb, io, sb, ps, j):
    st = sb["st"]
    xbj = sb["xb"][j % 2]
    ppz = ps["ppz"]
    for wsb, dst in ((sb["wf1b"], sb["z1b"]), (sb["wg1b"], sb["u1b"])):
        kb.w(nc.tensor, st["ppz_free"], st["xb_rdy"][j % 2])
        for fc in range(FC):
            ins = nc.tensor.matmul(
                ppz[:], wsb[:, fc * 128:(fc + 1) * 128],
                xbj[:, fc * TT1:(fc + 1) * TT1],
                start=(fc == 0), stop=(fc == FC - 1))
        t = kb.inc(ins, "eP")
        st["xb_use"][j % 2] = t
        kb.w(nc.scalar, t)
        cv = nc.scalar.copy(dst[:], ppz[:])
        st["ppz_free"] = kb.inc(cv, "eA")
    kb.w(nc.tensor, st["pb_free"])
    for fc in range(FC):
        ins = nc.tensor.matmul(
            ps["pb"][:], sb["wbb"][:, fc * 8:(fc + 1) * 8],
            xbj[:, fc * TT1:(fc + 1) * TT1],
            start=(fc == 0), stop=(fc == FC - 1))
    tb = kb.inc(ins, "eP")
    st["xb_use"][j % 2] = tb
    kb.w(nc.vector, tb, st["bfm_free"])
    ev = nc.vector.tensor_copy(sb["bfm"][:], ps["pb"][:])
    tf = kb.inc(ev, "eV")
    st["pb_free"] = tf
    kb.w(nc.tensor, tf, st["pbt_free"])
    for blk in range(4):
        ins = nc.tensor.transpose(
            ps["pbt"][:, blk * 8:(blk + 1) * 8],
            sb["bfm"][:, blk * 128:(blk + 1) * 128], sb["id8"][:])
    tt = kb.inc(ins, "eP")
    st["bfm_free"] = tt
    kb.w(nc.scalar, tt, st["bsig_free"])
    sg = nc.scalar.activation(sb["bsig"][:], ps["pbt"][:], AF.Sigmoid)
    ts = kb.inc(sg, "eA")
    st["pbt_free"] = ts
    kb.w(nc.sync, ts)
    d = nc.sync.dma_start(
        out=io["s_beta"][j * TT1:(j + 1) * TT1, :]
        .rearrange("(blk p) c -> p blk c", p=128),
        in_=sb["bsig"][:].rearrange("p (blk c) -> p blk c", blk=4))
    st["bsig_free"] = kb.inc(d, "qS")
    tga = None
    for blk in range(4):
        kb.w(nc.tensor, st["ppg_free"], st["ppz_free"])
        nc.tensor.matmul(ps["ppg"][0][:],
                         sb["z1b"][:, blk * 128:(blk + 1) * 128],
                         sb["wf2b"][:, 0:512], start=True, stop=True)
        i1 = nc.tensor.matmul(ps["ppg"][1][:],
                              sb["z1b"][:, blk * 128:(blk + 1) * 128],
                              sb["wf2b"][:, 512:1024], start=True, stop=True)
        tz = kb.inc(i1, "eP")
        kb.w(nc.scalar, tz, st["etmp_free"], st["gts_free"][blk])
        ex_i = None
        for half in range(2):
            ex_i = nc.scalar.activation(
                sb["etmp"][:, half * 512:(half + 1) * 512],
                ps["ppg"][half][:], AF.Exp, scale=-1.0)
        kb.w(nc.scalar, kb.inc(ex_i, "eA"))
        for half in range(2):
            ln0 = nc.scalar.activation(
                sb["gts"][:, blk * 1024 + half * 512:
                          blk * 1024 + (half + 1) * 512],
                sb["etmp"][:, half * 512:(half + 1) * 512], AF.Ln, bias=1.0)
        tg = kb.inc(ln0, "eA")
        st["ppg_free"] = tg
        st["etmp_free"] = tg
        kb.w(nc.tensor, tg)
        nc.tensor.matmul(ps["ppg"][0][:],
                         sb["u1b"][:, blk * 128:(blk + 1) * 128],
                         sb["wg2b"][:, 0:512], start=True, stop=True)
        i1 = nc.tensor.matmul(ps["ppg"][1][:],
                              sb["u1b"][:, blk * 128:(blk + 1) * 128],
                              sb["wg2b"][:, 512:1024], start=True, stop=True)
        tz = kb.inc(i1, "eP")
        kb.w(nc.vector, tz, st["gate_free"])
        for half in range(2):
            ga = nc.vector.tensor_add(
                sb["gatebuf"][:, blk * 1024 + half * 512:
                              blk * 1024 + (half + 1) * 512],
                ps["ppg"][half][:],
                sb["bg2bc"][:, half * 512:(half + 1) * 512])
        tga = kb.inc(ga, "eV")
        st["ppg_free"] = tga
    kb.w(nc.sync, tga)
    d = nc.sync.dma_start(
        out=io["s_gate"][j * TT1:(j + 1) * TT1, :]
        .rearrange("(blk p) m -> p blk m", p=128),
        in_=sb["gatebuf"][:].rearrange("p (blk m) -> p blk m", blk=4))
    st["gate_free"] = kb.inc(d, "qS")
    for blk in range(4):
        ch = j * 4 + blk
        kb.w(nc.tensor, st["ppg_free"], kb.t("eA"))
        nc.tensor.matmul(ps["ppc"][0][:], sb["m2t"][:],
                         sb["gts"][:, blk * 1024:blk * 1024 + 512],
                         start=True, stop=True)
        i1 = nc.tensor.matmul(ps["ppc"][1][:], sb["m2t"][:],
                              sb["gts"][:, blk * 1024 + 512:
                                         (blk + 1) * 1024],
                              start=True, stop=True)
        tc = kb.inc(i1, "eP")
        kb.w(nc.vector, tc, st["cgb_free"][blk % 2])
        for half in range(2):
            cg = nc.vector.tensor_copy(
                sb["cgb"][blk % 2][:, half * 512:(half + 1) * 512],
                ps["ppc"][half][:])
        tcg = kb.inc(cg, "eV")
        st["ppg_free"] = tcg
        kb.w(nc.sync, tcg)
        d = nc.sync.dma_start(out=io["s_cgrel"][ch * C:(ch + 1) * C, :],
                              in_=sb["cgb"][blk % 2][:])
        st["cgb_free"][blk % 2] = kb.inc(d, "qS")
        kb.w(nc.tensor, st["pbt_free"], st["ppd_free"])
        for h in range(CC):
            ins = nc.tensor.matmul(
                ps["ppd"][:, h * 3:(h + 1) * 3],
                sb["gts"][:, blk * 1024 + h * 128:blk * 1024 + (h + 1) * 128],
                sb["ones3n"][:], start=True, stop=True)
        td = kb.inc(ins, "eP")
        st["gts_free"][blk] = td
        kb.w(nc.scalar, td, st["dbuf_free"])
        de = nc.scalar.activation(sb["dbuf"][:], ps["ppd"][:, 0:24], AF.Exp)
        tde = kb.inc(de, "eA")
        st["ppd_free"] = tde
        st["pbt_free"] = tde
        kb.w(nc.sync, tde)
        d = nc.sync.dma_start(out=io["s_dcol"][ch], in_=sb["dbuf"][:])
        st["dbuf_free"] = kb.inc(d, "qS")


def emit_p1_pass(nc, kb, io, consts, pass_id, wcast, tpre):
    import contextlib
    with contextlib.ExitStack() as es:
        S = lambda n, sh, dt: es.enter_context(nc.sbuf_tensor(n, sh, dt))
        P_ = lambda n, sh, dt: es.enter_context(nc.psum_tensor(n, sh, dt))
        names = ("q", "k") if pass_id == 0 else ("v", "w")
        wins = ("wq", "wk") if pass_id == 0 else ("wv", "ww")
        tins = ("cq", "ck") if pass_id == 0 else ("cvs", "cvs")
        scrs = ("s_qf", "s_kf") if pass_id == 0 else ("s_vf", "s_wf")
        wsb = [S(f"w{pass_id}{i}", [128, FC * 1024], BF16) for i in range(2)]
        tapsb = [S(f"t{pass_id}{i}", [128, CC * 4], F32) for i in range(2)]
        stages = [[S(f"st{pass_id}{i}{bfn}", [128, CC * TT1], BF16)
                   for bfn in range(2)] for i in range(2)]
        xb = [S(f"xb{pass_id}{i}", [128, FC * TT1], BF16) for i in range(2)]
        ext = [S(f"ext{pass_id}{i}", [128, TT1 + 3], F32) for i in range(2)]
        ybuf = [S(f"y{pass_id}{i}", [128, TT1], F32) for i in range(2)]
        carry = S(f"carry{pass_id}", [128, 48], F32)
        pp = [P_(f"pp{pass_id}{i}", [128, TT1], F32) for i in range(2)]
        sb = {"xb": xb, "ext": ext, "ybuf": ybuf}
        ps = {"pp": pp}
        if pass_id == 1:
            sb["wf1b"] = S("wf1b", [128, FC * 128], BF16)
            sb["wg1b"] = S("wg1b", [128, FC * 128], BF16)
            sb["wbb"] = S("wbb", [128, FC * 8], BF16)
            sb["wf2b"] = S("wf2b", [128, 1024], BF16)
            sb["wg2b"] = S("wg2b", [128, 1024], BF16)
            sb["z1b"] = S("z1b", [128, TT1], BF16)
            sb["u1b"] = S("u1b", [128, TT1], BF16)
            sb["bfm"] = S("bfm", [8, TT1], F32)
            sb["bsig"] = S("bsig", [128, 32], F32)
            sb["etmp"] = S("etmp", [128, 1024], F32)
            sb["gts"] = S("gts", [128, 4096], F32)
            sb["gatebuf"] = S("gatebuf", [128, 4096], BF16)
            sb["cgb"] = [S(f"cgb{i}", [128, 1024], F32) for i in range(2)]
            sb["dbuf"] = S("dbuf", [128, 24], F32)
            sb["bg2bc"] = S("bg2bc", [128, 1024], F32)
            ps["ppz"] = P_("ppz", [128, TT1], F32)
            ps["pb"] = P_("pb", [8, TT1], F32)
            ps["pbt"] = P_("pbt", [128, 32], F32)
            ps["ppg"] = [P_(f"ppg{i}", [128, 512], F32) for i in range(2)]
            ps["ppc"] = ps["ppg"]
            ps["ppd"] = ps["pbt"]
        sb.update(consts)
        bwins = ("s_bwq", "s_bwk") if pass_id == 0 else ("s_bwv", "s_bww")
        kb.w(nc.sync, tpre[bwins[0]], tpre[bwins[1]], tpre["s_xb"])
        for i in range(2):
            tw = kb.dma(nc.sync, "qS",
                        out=wsb[i][:].rearrange("p (fc c) -> p fc c", fc=FC),
                        in_=io[bwins[i]].rearrange("(fc p) c -> p fc c",
                                                   p=128))
        if pass_id == 1:
            kb.w(nc.sync, tpre["s_bwf1"], tpre["s_bwg1"], tpre["s_bwbs"],
                 tpre["s_bwf2"], tpre["s_bwg2"])
            for nm, dst in (("s_bwf1", "wf1b"), ("s_bwg1", "wg1b"),
                            ("s_bwbs", "wbb")):
                tw = kb.dma(nc.sync, "qS",
                            out=sb[dst][:].rearrange("p (fc c) -> p fc c",
                                                     fc=FC),
                            in_=io[nm].rearrange("(fc p) c -> p fc c",
                                                 p=128))
            for nm, dst in (("s_bwf2", "wf2b"), ("s_bwg2", "wg2b")):
                tw = kb.dma(nc.sync, "qS", out=sb[dst][:], in_=io[nm])
        for i in range(2):
            ttap = kb.dma(nc.sync, "qS",
                          out=tapsb[i][:].rearrange("p (cc i) -> p cc i",
                                                    cc=CC),
                          in_=io[tins[i]].rearrange("(cc p) i -> p cc i",
                                                    p=128))
        t_xb0 = kb.dma(nc.sync, "qS",
                       out=xb[0][:].rearrange("p (fc t) -> p fc t", fc=FC),
                       in_=io["s_xb"][:, 0:TT1].rearrange(
                           "(fc p) t -> p fc t", p=128))
        cin = nc.vector.memset(carry[:], 0.0)
        tcr = kb.inc(cin, "eV")
        kb.w(nc.vector, ttap)
        st = {
            "pp_free": [("eA", 0), ("eA", 0)],
            "ext_free": [("eV", 0), ("eV", 0)],
            "y_free": [("eA", 0), ("eA", 0)],
            "stage_free": {(nm, bfn): ("qS", 0)
                           for nm in names for bfn in range(2)},
            "carry_w": {}, "carry": carry,
            "xb_rdy": [t_xb0, ("qS", 0)],
            "xb_use": [("eP", 0), ("eP", 0)],
            "w_rdy": tw,
            "ppz_free": ("eA", 0), "pb_free": ("eV", 0),
            "bfm_free": ("eP", 0), "pbt_free": ("eA", 0),
            "bsig_free": ("qS", 0), "ppg_free": ("eV", 0),
            "etmp_free": ("eA", 0), "gate_free": ("qS", 0),
            "cgb_free": [("qS", 0), ("qS", 0)],
            "ppc_free": ("eV", 0), "ppd_free": ("eA", 0),
            "dbuf_free": ("qS", 0),
            "gts_free": [("eP", 0)] * 4,
        }
        sb["st"] = st
        if pass_id == 1:
            with nc.sbuf_tensor("bg2row", [1, 1024], F32) as bg2row:
                d = nc.sync.dma_start(out=bg2row[:], in_=io["bg2s"])
                tb2 = kb.inc(d, "qS")
                kb.w(nc.tensor, tb2)
                nc.tensor.matmul(ps["ppg"][0][:], consts["onesrow"][:],
                                 bg2row[0:1, 0:512], start=True, stop=True)
                i1 = nc.tensor.matmul(ps["ppg"][1][:], consts["onesrow"][:],
                                      bg2row[0:1, 512:1024],
                                      start=True, stop=True)
                tp2 = kb.inc(i1, "eP")
                kb.w(nc.vector, tp2)
                nc.vector.tensor_copy(sb["bg2bc"][:, 0:512],
                                      ps["ppg"][0][:])
                cv2 = nc.vector.tensor_copy(sb["bg2bc"][:, 512:1024],
                                            ps["ppg"][1][:])
                st["ppg_free"] = kb.inc(cv2, "eV")
        projs = [{"name": nm, "w": wsb[i], "taps": tapsb[i],
                  "stage": stages[i], "scr": io[scrs[i]]}
                 for i, nm in enumerate(names)]
        for j in range(NT1):
            nxt = j + 1
            if nxt < NT1:
                kb.w(nc.sync, st["xb_use"][nxt % 2])
                st["xb_rdy"][nxt % 2] = kb.dma(
                    nc.sync, "qS",
                    out=xb[nxt % 2][:].rearrange("p (fc t) -> p fc t",
                                                 fc=FC),
                    in_=io["s_xb"][:, nxt * TT1:(nxt + 1) * TT1]
                    .rearrange("(fc p) t -> p fc t", p=128))
            emit_p1_projs(nc, kb, j, projs, xb, pp, ext, ybuf, st)
            if pass_id == 1:
                emit_p1_extras(nc, kb, io, sb, ps, j)
            wcast.step(3)



def emit_p2(nc, kb, io, consts, dyn):
    import contextlib
    with contextlib.ExitStack() as es:
        S = lambda n, sh, dt: es.enter_context(nc.sbuf_tensor(n, sh, dt))
        P_ = lambda n, sh, dt: es.enter_context(nc.psum_tensor(n, sh, dt))
        strm = {}
        for nm in ("qf", "kf", "vf", "wf"):
            strm[nm] = [S(f"p2{nm}{i}", [128, 1024], BF16) for i in range(2)]
        strm["cg"] = [S(f"p2cg{i}", [128, 1024], F32) for i in range(2)]
        strm["dc"] = [S(f"p2dc{i}", [128, 24], F32) for i in range(2)]
        strm["bt"] = [S(f"p2bt{i}", [128, 8], F32) for i in range(2)]
        strm["gt"] = [S(f"p2gt{i}", [128, 1024], BF16) for i in range(2)]
        k_tok = S("k_tok", [128, 1024], BF16)
        v_tok = S("v_tok", [128, 1024], BF16)
        U = S("U", [128, 1024], BF16)
        UT = S("UT", [128, 1024], BF16)
        at = S("at", [128, 1024], BF16)
        atT = S("atT", [128, 1024], BF16)
        PTm = S("PTm", [128, 1024], BF16)
        W2m = S("W2m", [128, 1024], BF16)
        skuT = S("skuT", [128, 1024], BF16)
        sku = S("sku", [128, 1024], BF16)
        svu = S("svu", [128, 1024], BF16)
        ogb = S("ogb", [128, 1024], BF16)
        ogf = S("ogf", [128, 1024], BF16)
        Arel = S("Arel", [128, 1024], F32)
        Ainv = S("Ainv", [128, 1024], BF16)
        sbf = S("sbf", [128, 1024], F32)
        pbf = S("pbf", [128, 1024], F32)
        wsq = S("wsq", [128, 1024], F32)
        sgg = S("sgg", [128, 1024], F32)
        SkT = S("SkT", [128, 1024], F32)
        Sv = S("Sv", [128, 1024], F32)
        ssum = S("ssum", [128, 8], F32)
        rl2 = S("rl2", [128, 8], F32)
        scl = S("scl", [128, 8], F32)
        negmax = S("negmax", [128, 8], F32)
        rsum = S("rsum", [128, 8], F32)
        rinv = S("rinv", [128, 8], F32)
        msum = S("msum", [128, 8], F32)
        rstd = S("rstd", [128, 8], F32)
        bfA = P_("bfA", [128, 1024], BF16)
        bfB = P_("bfB", [128, 1024], BF16)
        bfC = P_("bfC", [128, 1024], BF16)
        f32 = [P_(f"f32{i}", [128, 512], F32) for i in range(4)]
        trimask = consts["trimask"]
        idb = consts["idb"]

        nc.vector.memset(SkT[:], 0.0)
        iz = nc.vector.memset(Sv[:], 0.0)
        t0 = kb.inc(iz, "eV")

        t_p1 = kb.t("qS")
        fr = {"strm": [(t_p1,), (t_p1,)],
              "bfA": ("eV", 0), "bfB": ("eV", 0), "bfC": ("eV", 0),
              "f32AB": ("eV", 0), "f32CD": ("eV", 0),
              "ktok": ("eP", 0), "vtok": ("eP", 0), "U": ("eP", 0),
              "UT": ("eP", 0), "at": ("eP", 0), "atT": ("eP", 0),
              "PTm": ("eP", 0), "W2m": ("eP", 0), "sku": ("eP", 0),
              "skuT": ("eP", 0), "svu": ("eP", 0), "og": ("eP", 0),
              "ogf": ("qS", 0), "state": t0, "arel": ("eV", 0),
              "s": ("eA", 0), "p": ("eV", 0), "wsq": ("eV", 0),
              "sgg": ("eV", 0)}
        strm_rdy = [("qS", 0), ("qS", 0)]

        def load_chunk(c):
            bf = c % 2
            kb.w(nc.sync, *fr["strm"][bf])
            for nm, scr in (("qf", "s_qf"), ("kf", "s_kf"), ("vf", "s_vf"),
                            ("wf", "s_wf")):
                kb.dma(nc.sync, "qS",
                       out=strm[nm][bf][:].rearrange("p (h t) -> p h t",
                                                     h=8),
                       in_=io[scr][:, c * C:(c + 1) * C].rearrange(
                           "(h p) t -> p h t", p=128))
            kb.dma(nc.sync, "qS", out=strm["cg"][bf][:],
                   in_=io["s_cgrel"][c * C:(c + 1) * C, :])
            kb.dma(nc.sync, "qS", out=strm["dc"][bf][:],
                   in_=io["s_dcol"][c])
            kb.dma(nc.sync, "qS", out=strm["bt"][bf][:],
                   in_=io["s_beta"][c * C:(c + 1) * C, :])
            strm_rdy[bf] = kb.dma(nc.sync, "qS", out=strm["gt"][bf][:],
                                  in_=io["s_gate"][c * C:(c + 1) * C, :])

        load_chunk(0)
        load_chunk(1)
        H128 = lambda t_, h: t_[:, h * 128:(h + 1) * 128]
        for c in range(NCH):
            bf = c % 2
            qf, kf, vf, wf = (strm[x][bf] for x in ("qf", "kf", "vf", "wf"))
            cg, dc, bt, gt = (strm[x][bf] for x in ("cg", "dc", "bt", "gt"))
            kb.w(nc.tensor, strm_rdy[bf], fr["bfA"], fr["bfB"], fr["bfC"])
            for h in range(8):
                nc.tensor.transpose(H128(bfA, h), H128(kf, h), idb[:])
            for h in range(8):
                nc.tensor.transpose(H128(bfB, h), H128(vf, h), idb[:])
            for h in range(8):
                i_ = nc.tensor.transpose(H128(bfC, h), H128(wf, h), idb[:])
            tT = kb.inc(i_, "eP")
            kb.w(nc.vector, tT, fr["ktok"], fr["vtok"])
            nc.vector.tensor_copy(k_tok[:], bfA[:])
            ev = nc.vector.tensor_copy(v_tok[:], bfB[:])
            tkv = kb.inc(ev, "eV")
            fr["bfA"] = tkv
            fr["bfB"] = tkv
            kb.w(nc.scalar, strm_rdy[bf], fr["arel"], fr["sgg"])
            nc.scalar.activation(Arel[:], cg[:], AF.Exp)
            nc.scalar.activation(Ainv[:], cg[:], AF.Exp, scale=-1.0)
            nc.scalar.activation(sgg[:], gt[:], AF.Sigmoid)
            kb.w(nc.scalar, tT, fr["wsq"])
            sq = nc.scalar.activation(wsq[:], bfC[:], AF.Square)
            tsq = kb.inc(sq, "eA")
            kb.w(nc.vector, tsq)
            rd = nc.vector.tensor_reduce(
                ssum[:], wsq[:].rearrange("p (h m) -> p h m", h=8),
                AX.X, ALU.add)
            tss = kb.inc(rd, "eV")
            fr["wsq"] = tss
            kb.w(nc.scalar, tss)
            ln_i = nc.scalar.activation(rl2[:], ssum[:], AF.Ln, bias=1e-6)
            kb.w(nc.scalar, kb.inc(ln_i, "eA"))
            ee = nc.scalar.activation(rl2[:], rl2[:], AF.Exp, scale=-0.5)
            te = kb.inc(ee, "eA")
            kb.w(nc.vector, te)
            sc = nc.vector.tensor_mul(scl[:], rl2[:], bt[:])
            tscl = kb.inc(sc, "eV")
            kb.w(nc.vector, tscl)
            kb.w(nc.vector, tscl, kb.t("eA"), fr["U"])
            for h in range(8):
                uu = nc.vector.scalar_tensor_tensor(
                    H128(U, h), H128(bfC, h), scl[:, h:h + 1], H128(Ainv, h),
                    ALU.mult, ALU.mult)
            tU = kb.inc(uu, "eV")
            fr["bfC"] = tU
            kb.w(nc.tensor, fr["f32AB"], strm_rdy[bf])
            for h in range(8):
                pt_i = nc.tensor.matmul(
                    f32[h // 4][:, (h % 4) * 128:(h % 4 + 1) * 128],
                    H128(kf, h), H128(qf, h), start=True, stop=True)
            tPT = kb.inc(pt_i, "eP")
            kb.w(nc.vector, tPT, fr["PTm"])
            nc.vector.tensor_tensor(PTm[:, 0:512], f32[0][:], trimask[:],
                                    ALU.mult)
            pe_i = nc.vector.tensor_tensor(PTm[:, 512:1024], f32[1][:],
                                           trimask[:], ALU.mult)
            tPTm = kb.inc(pe_i, "eV")
            fr["f32AB"] = tPTm
            kb.w(nc.vector, fr["state"], strm_rdy[bf], fr["skuT"], fr["svu"])
            for h in range(8):
                nc.vector.tensor_scalar(
                    H128(skuT, h), H128(SkT, h), dc[:, h * 3 + 2:h * 3 + 3],
                    None, ALU.mult)
            for h in range(8):
                sv_i = nc.vector.tensor_scalar(
                    H128(svu, h), H128(Sv, h), dc[:, h * 3 + 2:h * 3 + 3],
                    None, ALU.mult)
            tsv = kb.inc(sv_i, "eV")
            kb.w(nc.tensor, tU, tsv, tkv)
            for h in range(8):
                nc.tensor.transpose(H128(bfC, h), H128(U, h), idb[:])
            for h in range(8):
                tr_i = nc.tensor.transpose(H128(bfA, h), H128(skuT, h),
                                           idb[:])
            tT2 = kb.inc(tr_i, "eP")
            fr["skuT"] = tT2
            kb.w(nc.vector, tT2, fr["UT"], fr["sku"])
            nc.vector.tensor_copy(UT[:], bfC[:])
            ev_i = nc.vector.tensor_copy(sku[:], bfA[:])
            tUT = kb.inc(ev_i, "eV")
            fr["bfC"] = tUT
            fr["bfA"] = tUT
            kb.w(nc.tensor, tPTm, tUT, fr["f32AB"])
            for h in range(8):
                nc.tensor.matmul(
                    f32[h // 4][:, (h % 4) * 128:(h % 4 + 1) * 128],
                    H128(PTm, h), H128(U, h), start=True, stop=False)
                s_i = nc.tensor.matmul(
                    f32[h // 4][:, (h % 4) * 128:(h % 4 + 1) * 128],
                    H128(qf, h), H128(sku, h), start=False, stop=True)
            tS = kb.inc(s_i, "eP")
            fr["PTm"] = tS
            fr["sku"] = tS
            kb.w(nc.vector, tS, kb.t("eA"), fr["s"])
            nc.vector.scalar_tensor_tensor(sbf[:, 0:512], f32[0][:], SCALE,
                                           Arel[:, 0:512], ALU.mult,
                                           ALU.mult)
            st_i = nc.vector.scalar_tensor_tensor(
                sbf[:, 512:1024], f32[1][:], SCALE, Arel[:, 512:1024],
                ALU.mult, ALU.mult)
            kb.w(nc.vector, kb.inc(st_i, "eV"))
            nm_i = nc.vector.reduce_max(
                negmax[:], sbf[:].rearrange("p (h m) -> p h m", h=8),
                AX.X, negate=True)
            tsb = kb.inc(nm_i, "eV")
            fr["f32AB"] = tsb
            kb.w(nc.scalar, tsb, fr["p"])
            for h in range(8):
                p_i = nc.scalar.activation(
                    H128(pbf, h), H128(sbf, h), AF.Exp,
                    bias=negmax[:, h:h + 1], accum_out=rsum[:, h:h + 1])
            tp = kb.inc(p_i, "eA")
            fr["s"] = tp
            kb.w(nc.vector, tp, fr["at"])
            rc_i = nc.vector.reciprocal(rinv[:], rsum[:])
            kb.w(nc.vector, kb.inc(rc_i, "eV"))
            for h in range(8):
                at_i = nc.vector.scalar_tensor_tensor(
                    H128(at, h), H128(pbf, h), rinv[:, h:h + 1],
                    H128(Arel, h), ALU.mult, ALU.mult)
            tat = kb.inc(at_i, "eV")
            fr["p"] = tat
            fr["arel"] = tat
            kb.w(nc.tensor, tat, fr["bfB"])
            for h in range(8):
                t_i = nc.tensor.transpose(H128(bfB, h), H128(at, h), idb[:])
            tatT = kb.inc(t_i, "eP")
            fr["at"] = tatT
            kb.w(nc.vector, tatT, fr["atT"])
            ev_i = nc.vector.tensor_copy(atT[:], bfB[:])
            tatTs = kb.inc(ev_i, "eV")
            fr["bfB"] = tatTs
            kb.w(nc.tensor, tatTs, tUT, fr["f32AB"])
            for h in range(8):
                po = f32[h // 4]
                co = (h % 4) * 128
                nc.tensor.matmul(po[0:64, co:co + 64],
                                 UT[:, h * 128:h * 128 + 64],
                                 atT[:, h * 128:h * 128 + 64],
                                 start=True, stop=True)
                nc.tensor.matmul(po[0:64, co + 64:co + 128],
                                 UT[:, h * 128:h * 128 + 64],
                                 atT[:, h * 128 + 64:(h + 1) * 128],
                                 start=True, stop=True)
                w2_i = nc.tensor.matmul(
                    po[64:128, co + 64:co + 128],
                    UT[:, h * 128 + 64:(h + 1) * 128],
                    atT[:, h * 128 + 64:(h + 1) * 128],
                    start=True, stop=True)
            tW2 = kb.inc(w2_i, "eP")
            fr["UT"] = tW2
            kb.w(nc.vector, tW2, fr["W2m"])
            nc.vector.tensor_tensor(W2m[:, 0:512], f32[0][:], trimask[:],
                                    ALU.mult)
            w_i = nc.vector.tensor_tensor(W2m[:, 512:1024], f32[1][:],
                                          trimask[:], ALU.mult)
            tW2m = kb.inc(w_i, "eV")
            fr["f32AB"] = tW2m
            kb.w(nc.tensor, tW2m, fr["f32AB"])
            for h in range(8):
                nc.tensor.matmul(
                    f32[h // 4][:, (h % 4) * 128:(h % 4 + 1) * 128],
                    H128(W2m, h), H128(v_tok, h), start=True, stop=False)
                o_i = nc.tensor.matmul(
                    f32[h // 4][:, (h % 4) * 128:(h % 4 + 1) * 128],
                    H128(atT, h), H128(svu, h), start=False, stop=True)
            tO = kb.inc(o_i, "eP")
            fr["W2m"] = tO
            fr["atT"] = tO
            fr["svu"] = tO
            kb.w(nc.scalar, tO, fr["wsq"])
            nc.scalar.activation(wsq[:, 0:512], f32[0][:], AF.Square)
            sq_i = nc.scalar.activation(wsq[:, 512:1024], f32[1][:],
                                        AF.Square)
            tsq2 = kb.inc(sq_i, "eA")
            kb.w(nc.vector, tsq2)
            rd_i = nc.vector.tensor_reduce(
                msum[:], wsq[:].rearrange("p (h m) -> p h m", h=8),
                AX.X, ALU.add)
            kb.w(nc.vector, kb.inc(rd_i, "eV"))
            ms_i = nc.vector.tensor_scalar(msum[:], msum[:], 1.0 / 128,
                                           1e-5, ALU.mult, ALU.add)
            tms = kb.inc(ms_i, "eV")
            fr["wsq"] = tms
            kb.w(nc.scalar, tms)
            ln_i = nc.scalar.activation(rstd[:], msum[:], AF.Ln)
            kb.w(nc.scalar, kb.inc(ln_i, "eA"))
            rs_i = nc.scalar.activation(rstd[:], rstd[:], AF.Exp, scale=-0.5)
            trs = kb.inc(rs_i, "eA")
            kb.w(nc.vector, trs, fr["og"])
            for h in range(8):
                og_i = nc.vector.scalar_tensor_tensor(
                    H128(ogb, h),
                    f32[h // 4][:, (h % 4) * 128:(h % 4 + 1) * 128],
                    rstd[:, h:h + 1], H128(sgg, h), ALU.mult, ALU.mult)
            tog = kb.inc(og_i, "eV")
            fr["f32AB"] = tog
            fr["sgg"] = tog
            kb.w(nc.tensor, tog, fr["bfA"])
            for h in range(8):
                t_i = nc.tensor.transpose(H128(bfA, h), H128(ogb, h), idb[:])
            togT = kb.inc(t_i, "eP")
            fr["og"] = togT
            kb.w(nc.vector, togT, fr["ogf"])
            ev_i = nc.vector.tensor_copy(ogf[:], bfA[:])
            togf = kb.inc(ev_i, "eV")
            fr["bfA"] = togf
            kb.w(nc.sync, togf)
            if not _DEV.get("skip_ogstore"):
                fr["ogf"] = kb.dma(
                    nc.sync, "qS",
                    out=io["s_og"][:, c * C:(c + 1) * C].rearrange(
                        "(h p) t -> p h t", p=128),
                    in_=ogf[:].rearrange("p (h t) -> p h t", h=8))
            kb.w(nc.tensor, fr["f32CD"], tU)
            for h in range(8):
                dk_i = nc.tensor.matmul(
                    f32[2 + h // 4][:, (h % 4) * 128:(h % 4 + 1) * 128],
                    H128(U, h), H128(k_tok, h), start=True, stop=True)
            tdk = kb.inc(dk_i, "eP")
            fr["ktok"] = tdk
            kb.w(nc.scalar, tsv, strm_rdy[bf])
            for h in range(8):
                dec_i = nc.scalar.activation(
                    H128(SkT, h), H128(SkT, h), AF.Copy,
                    scale=dc[:, h * 3 + 0:h * 3 + 1])
            tdec = kb.inc(dec_i, "eA")
            kb.w(nc.vector, tdec, tdk)
            for h in range(8):
                up_i = nc.vector.scalar_tensor_tensor(
                    H128(SkT, h),
                    f32[2 + h // 4][:, (h % 4) * 128:(h % 4 + 1) * 128],
                    dc[:, h * 3 + 1:h * 3 + 2], H128(SkT, h),
                    ALU.mult, ALU.add)
            tsk = kb.inc(up_i, "eV")
            fr["f32CD"] = tsk
            kb.w(nc.tensor, tsk, tO)
            for h in range(8):
                dv_i = nc.tensor.matmul(
                    f32[2 + h // 4][:, (h % 4) * 128:(h % 4 + 1) * 128],
                    H128(U, h), H128(v_tok, h), start=True, stop=True)
            tdv = kb.inc(dv_i, "eP")
            fr["U"] = tdv
            fr["vtok"] = tdv
            kb.w(nc.scalar, tsk)
            for h in range(8):
                dec_i = nc.scalar.activation(
                    H128(Sv, h), H128(Sv, h), AF.Copy,
                    scale=dc[:, h * 3 + 0:h * 3 + 1])
            tdec = kb.inc(dec_i, "eA")
            kb.w(nc.vector, tdec, tdv)
            for h in range(8):
                up_i = nc.vector.scalar_tensor_tensor(
                    H128(Sv, h),
                    f32[2 + h // 4][:, (h % 4) * 128:(h % 4 + 1) * 128],
                    dc[:, h * 3 + 1:h * 3 + 2], H128(Sv, h),
                    ALU.mult, ALU.add)
            tsv2 = kb.inc(up_i, "eV")
            fr["f32CD"] = tsv2
            fr["state"] = tsv2
            fr["strm"][bf] = (tdv, tsv2)
            if _DEV.get("p2dump") == c:
                dumps = [k_tok, v_tok, U, Arel, Ainv, PTm, sku, svu, sbf,
                         pbf, at, atT, UT, W2m, ogb, SkT, Sv, sgg, None,
                         skuT]
                kb.w(nc.gpsimd, kb.t("eV"), kb.t("eA"), kb.t("eP"))
                for di, tsr in enumerate(dumps):
                    if tsr is None:
                        continue
                    tgt = (io["p2dbg"] if tsr.tensor.dtype == F32
                           else io["p2dbgB"])
                    kb.dma(nc.gpsimd, "qG", out=tgt[di], in_=tsr[:])
                for di2, (tsr, w_) in enumerate(
                        ((rsum, 8), (rinv, 8), (negmax, 8), (rstd, 8),
                         (scl, 8), (ssum, 8), (msum, 8), (rl2, 8))):
                    kb.dma(nc.gpsimd, "qG",
                           out=io["p2dbg"][18][:, di2 * 8:di2 * 8 + w_],
                           in_=tsr[:])
                for eng in (nc.tensor, nc.vector, nc.scalar, nc.sync):
                    kb.w(eng, kb.t("qG"))
            if c + 2 < NCH:
                load_chunk(c + 2)



def emit_ln_fm(nc, kb, xr, x1b, tmpA, rows, pr1, pr2, pbc1, pbc2,
               onesrow, onescol1, fr):
    kb.w(nc.tensor, kb.t("eV"), fr["pr"])
    for fc in range(FC):
        nc.tensor.matmul(pr1[:], onescol1[:],
                         xr[:, fc * TT3:(fc + 1) * TT3],
                         start=(fc == 0), stop=(fc == FC - 1))
    for fc in range(FC):
        qb = fc % 2
        kb.w(nc.scalar, fr["tmpA"][qb], kb.t("eV"))
        sq = nc.scalar.activation(tmpA[qb][:],
                                  xr[:, fc * TT3:(fc + 1) * TT3], AF.Square)
        t_sq = kb.inc(sq, "eA")
        kb.w(nc.tensor, t_sq)
        s2_i = nc.tensor.matmul(pr2[:], onescol1[:], tmpA[qb][:],
                                start=(fc == 0), stop=(fc == FC - 1))
        fr["tmpA"][qb] = kb.inc(s2_i, "eP")
    t_stat = kb.t("eP")
    mu = rows[0:1, 0:TT3]
    m2 = rows[0:1, TT3:2 * TT3]
    var = rows[0:1, 2 * TT3:3 * TT3]
    rstd = rows[0:1, 3 * TT3:4 * TT3]
    kb.w(nc.vector, t_stat, fr["rows"])
    nc.vector.tensor_scalar(mu, pr1[:], 1.0 / HID, None, ALU.mult)
    m2_i = nc.vector.tensor_scalar(m2, pr2[:], 1.0 / HID, None, ALU.mult)
    kb.w(nc.vector, kb.inc(m2_i, "eV"))
    mv_i = nc.vector.tensor_mul(var, mu, mu)
    kb.w(nc.vector, kb.inc(mv_i, "eV"))
    sb_i = nc.vector.tensor_sub(var, m2, var)
    t_v = kb.inc(sb_i, "eV")
    fr["pr"] = t_v
    kb.w(nc.scalar, t_v)
    ln_i = nc.scalar.activation(rstd, var, AF.Ln, bias=1e-5)
    kb.w(nc.scalar, kb.inc(ln_i, "eA"))
    r_i = nc.scalar.activation(rstd, rstd, AF.Exp, scale=-0.5)
    t_r = kb.inc(r_i, "eA")
    kb.w(nc.tensor, t_r, fr["pbc"])
    nc.tensor.matmul(pbc1[:], onesrow[:], mu, start=True, stop=True)
    b_i = nc.tensor.matmul(pbc2[:], onesrow[:], rstd, start=True, stop=True)
    t_b = kb.inc(b_i, "eP")
    fr["rows"] = t_b
    kb.w(nc.vector, t_b)
    c_i = None
    for fc in range(FC):
        sl = xr[:, fc * TT3:(fc + 1) * TT3]
        nc.vector.tensor_sub(sl, sl, pbc1[:])
        nc.vector.tensor_mul(sl, sl, pbc2[:])
        if x1b is not None:
            c_i = nc.vector.tensor_copy(x1b[:, fc * TT3:(fc + 1) * TT3], sl)
        else:
            c_i = nc.vector.engine_nop()
    t_x1 = kb.inc(c_i, "eV")
    fr["pbc"] = t_x1
    return t_x1


def emit_p3(nc, kb, io, consts, dyn, t_ar, t_wall):
    import contextlib
    with contextlib.ExitStack() as es:
        S = lambda n, sh, dt: es.enter_context(nc.sbuf_tensor(n, sh, dt))
        P_ = lambda n, sh, dt: es.enter_context(nc.psum_tensor(n, sh, dt))
        omy = S("omy", [128, FC * TT3], BF16)
        xr = S("xr", [128, FC * TT3], F32)
        x1b = S("x1b", [128, FC * TT3], BF16)
        act = S("act", [128, IC_SH * TT3], BF16)
        act2 = S("act2", [128, IC_RT * TT3], BF16)
        wts = [S(f"wts{i}", [128, FC * 128], BF16) for i in range(2)]
        gw = [S(f"gw{i}", [128, FC * 128], BF16) for i in range(2)]
        uw = [S(f"uw{i}", [128, FC * 128], BF16) for i in range(2)]
        dw = [S(f"dw{i}", [128, 32 * 128], BF16) for i in range(2)]
        rdw = S("rdw", [128, 8 * 128], BF16)
        tmpA = [S(f"tmpA{i}", [128, TT3], F32) for i in range(2)]
        tmpB = [S(f"tmpB{i}", [128, TT3], F32) for i in range(2)]
        r2y = [S(f"r2y{i}", [128, TT3], F32) for i in range(2)]
        rows = S("rows", [1, 4 * TT3], F32)
        pg = [P_(f"pg{i}", [128, TT3], F32) for i in range(2)]
        pu = [P_(f"pu{i}", [128, TT3], F32) for i in range(2)]
        pd = [P_(f"pd{i}", [128, TT3], F32) for i in range(2)]
        pr1 = P_("pr1", [1, TT3], F32)
        pr2 = P_("pr2", [1, TT3], F32)
        onesrow = consts["onesrow"]
        onescol1 = consts["onescol1"]

        fr = {"wts": [("eP", 0)] * 2, "gw": [("eP", 0)] * 2,
              "uw": [("eP", 0)] * 2, "dw": [("eP", 0)] * 2,
              "rdw": ("eP", 0),
              "pg": [("eA", 0)] * 2, "pu": [("eV", 0)] * 2,
              "pd": [("eV", 0)] * 2,
              "tmpA": [("eP", 0)] * 2, "tmpB": [("qS", 0)] * 2,
              "r2y": [(("eP", 0), ("qS", 0))] * 2,
              "omy": ("eP", 0), "xr": ("eV", 0), "x1b": ("eP", 0),
              "act": ("eP", 0), "act2": ("eP", 0),
              "rows": ("eP", 0), "pr": ("eV", 0), "pbc": ("eV", 0)}

        for tj in range(NT3):
            cb = dyn["p3"] + tj * TT3
            kb.w(nc.sync, t_ar, fr["omy"], fr["xr"], kb.t("eV"))
            kb.dma(nc.sync, "qS",
                   out=omy[:].rearrange("p (oc t) -> p oc t", oc=FC),
                   in_=io["s_ored"][:, DS(cb, TT3)].rearrange(
                       "(oc p) t -> p oc t", p=128))
            t_in = kb.dma(nc.sync, "qS",
                          out=xr[:].rearrange("p (fc t) -> p fc t", fc=FC),
                          in_=io["xt"][:, DS(cb, TT3)].rearrange(
                              "(fc p) t -> p fc t", p=128))
            t_r1 = None
            for yc in range(FC):
                wb = yc % 2
                kb.w(nc.scalar, fr["wts"][wb], t_wall)
                d = nc.scalar.dma_start(
                    out=wts[wb][:].rearrange("p (oc c) -> p oc c", oc=FC),
                    in_=io["s_bwo"][:, yc * 128:(yc + 1) * 128].rearrange(
                        "(oc p) c -> p oc c", p=128))
                t_w = kb.inc(d, "qA")
                kb.w(nc.tensor, t_w, t_in, fr["pd"][wb])
                for oc in range(FC):
                    mm = nc.tensor.matmul(
                        pd[wb][:], wts[wb][:, oc * 128:(oc + 1) * 128],
                        omy[:, oc * TT3:(oc + 1) * TT3],
                        start=(oc == 0), stop=(oc == FC - 1))
                t_mm = kb.inc(mm, "eP")
                fr["wts"][wb] = t_mm
                kb.w(nc.vector, t_mm)
                ad = nc.vector.tensor_add(
                    xr[:, yc * TT3:(yc + 1) * TT3], pd[wb][:],
                    xr[:, yc * TT3:(yc + 1) * TT3])
                t_r1 = kb.inc(ad, "eV")
                fr["pd"][wb] = t_r1
            fr["omy"] = kb.t("eP")
            kb.w(nc.vector, kb.t("eP"))
            t_x1 = emit_ln_fm(nc, kb, xr, x1b, tmpA, rows, pr1, pr2,
                              pg[0], pu[0], onesrow, onescol1, fr)
            fr["pg"][0] = t_x1
            fr["pu"][0] = t_x1
            for nic, gsrc, usrc, actb, akey in (
                    (IC_SH, "s_bshg", "s_bshu", act, "act"),
                    (IC_RT, "s_brg", "s_bru", act2, "act2")):
                for ic in range(nic):
                    wb = ic % 2
                    kb.w(nc.scalar, fr["gw"][wb], fr["uw"][wb], t_wall)
                    kb.dma(nc.scalar, "qA",
                           out=gw[wb][:].rearrange("p (fc c) -> p fc c",
                                                   fc=FC),
                           in_=io[gsrc][:, ic * 128:(ic + 1) * 128]
                           .rearrange("(fc p) c -> p fc c", p=128))
                    t_w = kb.dma(nc.scalar, "qA",
                                 out=uw[wb][:].rearrange(
                                     "p (fc c) -> p fc c", fc=FC),
                                 in_=io[usrc][:, ic * 128:(ic + 1) * 128]
                                 .rearrange("(fc p) c -> p fc c", p=128))
                    kb.w(nc.tensor, t_w, t_x1, fr["pg"][wb], fr["pu"][wb])
                    for fc in range(FC):
                        nc.tensor.matmul(
                            pg[wb][:], gw[wb][:, fc * 128:(fc + 1) * 128],
                            x1b[:, fc * TT3:(fc + 1) * TT3],
                            start=(fc == 0), stop=(fc == FC - 1))
                    for fc in range(FC):
                        mm = nc.tensor.matmul(
                            pu[wb][:], uw[wb][:, fc * 128:(fc + 1) * 128],
                            x1b[:, fc * TT3:(fc + 1) * TT3],
                            start=(fc == 0), stop=(fc == FC - 1))
                    t_mm = kb.inc(mm, "eP")
                    fr["gw"][wb] = t_mm
                    fr["uw"][wb] = t_mm
                    kb.w(nc.scalar, t_mm, fr["tmpB"][wb])
                    si = nc.scalar.activation(tmpB[wb][:], pg[wb][:],
                                              AF.Silu)
                    t_si = kb.inc(si, "eA")
                    fr["pg"][wb] = t_si
                    kb.w(nc.vector, t_si, fr[akey])
                    mu_i = nc.vector.tensor_mul(
                        actb[:, ic * TT3:(ic + 1) * TT3], tmpB[wb][:],
                        pu[wb][:])
                    t_mu = kb.inc(mu_i, "eV")
                    fr["pu"][wb] = t_mu
                    fr["tmpB"][wb] = t_mu
            t_act = kb.t("eV")
            kb.w(nc.scalar, fr["rdw"])
            d = nc.scalar.dma_start(
                out=rdw[:].rearrange("p (ic c) -> p ic c", ic=8),
                in_=io["s_brd"][:, 0:128].rearrange(
                    "(ic p) c -> p ic c", p=128))
            t_rdw0 = kb.inc(d, "qA")
            t_rdw = t_rdw0
            rdw_yc = 0
            for yc in range(FC):
                hb0, hb1 = yc % 2, (yc + 1) % 2
                for half in range(2):
                    hb = (yc * 2 + half) % 2
                    kb.w(nc.scalar, fr["dw"][hb])
                    d = nc.scalar.dma_start(
                        out=dw[hb][:].rearrange("p (ic c) -> p ic c", ic=32),
                        in_=io["s_bshd"][half * 4096:(half + 1) * 4096,
                                         yc * 128:(yc + 1) * 128].rearrange(
                            "(ic p) c -> p ic c", p=128))
                    t_dw = kb.inc(d, "qA")
                    kb.w(nc.tensor, t_dw, t_act, fr["pd"][yc % 2]
                         if half == 0 else ("eP", 0))
                    for ic in range(32):
                        gic = half * 32 + ic
                        mm = nc.tensor.matmul(
                            pd[yc % 2][:],
                            dw[hb][:, ic * 128:(ic + 1) * 128],
                            act[:, gic * TT3:(gic + 1) * TT3],
                            start=(gic == 0), stop=False)
                    fr["dw"][hb] = kb.inc(mm, "eP")
                if rdw_yc != yc:
                    pass
                kb.w(nc.tensor, t_rdw)
                for ic in range(IC_RT):
                    mm = nc.tensor.matmul(
                        pd[yc % 2][:], rdw[:, ic * 128:(ic + 1) * 128],
                        act2[:, ic * TT3:(ic + 1) * TT3],
                        start=False, stop=(ic == IC_RT - 1))
                t_d = kb.inc(mm, "eP")
                if yc + 1 < FC:
                    kb.w(nc.scalar, t_d)
                    d = nc.scalar.dma_start(
                        out=rdw[:].rearrange("p (ic c) -> p ic c", ic=8),
                        in_=io["s_brd"][:, (yc + 1) * 128:(yc + 2) * 128]
                        .rearrange("(ic p) c -> p ic c", p=128))
                    t_rdw = kb.inc(d, "qA")
                rb = yc % 2
                kb.w(nc.vector, t_d, *fr["r2y"][rb])
                r2_i = nc.vector.tensor_add(
                    r2y[rb][:], pd[yc % 2][:],
                    xr[:, yc * TT3:(yc + 1) * TT3])
                t_r2 = kb.inc(r2_i, "eV")
                fr["pd"][yc % 2] = t_r2
                kb.w(nc.tensor, t_r2)
                nc.tensor.matmul(pr1[:], onescol1[:], r2y[rb][:],
                                 start=(yc == 0), stop=(yc == FC - 1))
                kb.w(nc.scalar, t_r2, fr["tmpA"][rb])
                sq = nc.scalar.activation(tmpA[rb][:], r2y[rb][:], AF.Square)
                t_sq = kb.inc(sq, "eA")
                kb.w(nc.tensor, t_sq)
                s2_i = nc.tensor.matmul(pr2[:], onescol1[:], tmpA[rb][:],
                                        start=(yc == 0), stop=(yc == FC - 1))
                t_st = kb.inc(s2_i, "eP")
                fr["tmpA"][rb] = t_st
                kb.w(nc.sync, t_r2)
                d = nc.sync.dma_start(
                    out=io["s_r2"][yc * 128:(yc + 1) * 128,
                                   tj * TT3:(tj + 1) * TT3],
                    in_=r2y[rb][:])
                t_r2s = kb.inc(d, "qS")
                fr["r2y"][rb] = (t_st, t_r2s)
            t_stats = kb.t("eP")
            t_r2all = kb.t("qS")
            mu = rows[0:1, 0:TT3]
            m2 = rows[0:1, TT3:2 * TT3]
            var = rows[0:1, 2 * TT3:3 * TT3]
            rstd = rows[0:1, 3 * TT3:4 * TT3]
            kb.w(nc.vector, t_stats, fr["rows"])
            nc.vector.tensor_scalar(mu, pr1[:], 1.0 / HID, None, ALU.mult)
            m2_i = nc.vector.tensor_scalar(m2, pr2[:], 1.0 / HID, None,
                                           ALU.mult)
            kb.w(nc.vector, kb.inc(m2_i, "eV"))
            mv_i = nc.vector.tensor_mul(var, mu, mu)
            kb.w(nc.vector, kb.inc(mv_i, "eV"))
            sb_i = nc.vector.tensor_sub(var, m2, var)
            t_v = kb.inc(sb_i, "eV")
            fr["pr"] = t_v
            kb.w(nc.scalar, t_v)
            ln_i = nc.scalar.activation(rstd, var, AF.Ln, bias=1e-5)
            kb.w(nc.scalar, kb.inc(ln_i, "eA"))
            r_i = nc.scalar.activation(rstd, rstd, AF.Exp, scale=-0.5)
            t_r = kb.inc(r_i, "eA")
            kb.w(nc.tensor, t_r, fr["pbc"], kb.t("eA"))
            nc.tensor.matmul(pg[0][:], onesrow[:], mu, start=True, stop=True)
            b_i = nc.tensor.matmul(pu[0][:], onesrow[:], rstd,
                                   start=True, stop=True)
            t_b = kb.inc(b_i, "eP")
            fr["rows"] = t_b
            for fc in range(FC):
                rb = fc % 2
                kb.w(nc.sync, t_r2all, fr["tmpA"][rb], kb.t("eV"))
                d = nc.sync.dma_start(
                    out=tmpA[rb][:],
                    in_=io["s_r2"][fc * 128:(fc + 1) * 128,
                                   tj * TT3:(tj + 1) * TT3])
                t_ld = kb.inc(d, "qS")
                kb.w(nc.vector, t_ld, t_b, fr["tmpB"][rb])
                nc.vector.tensor_sub(tmpB[rb][:], tmpA[rb][:], pg[0][:])
                o_i = nc.vector.tensor_mul(tmpB[rb][:], tmpB[rb][:],
                                           pu[0][:])
                t_o = kb.inc(o_i, "eV")
                fr["tmpA"][rb] = t_o
                kb.w(nc.sync, t_o)
                d = nc.sync.dma_start(
                    out=io["outf"][fc * 128:(fc + 1) * 128,
                                   tj * TT3:(tj + 1) * TT3],
                    in_=tmpB[rb][:])
                t_os = kb.inc(d, "qS")
                fr["tmpB"][rb] = t_os
            fr["pbc"] = kb.t("eV")
            fr["x1b"] = kb.t("eP")
            fr["xr"] = kb.t("eV")
            fr["act"] = kb.t("eP")
            fr["act2"] = kb.t("eP")



class WCast:

    def __init__(self, nc, kb, io):
        self.nc, self.kb, self.io = nc, kb, io
        self.jobs = []
        for src, dst, rows in (("wo", None, 0),):
            pass
        for src, dst, rows in (
                ("shgT", "s_bshg", HID), ("shuT", "s_bshu", HID),
                ("shdT", "s_bshd", 8192), ("rgT", "s_brg", HID),
                ("ruT", "s_bru", HID), ("rdT", "s_brd", 1024)):
            step = 512
            for r in range(0, rows, step):
                self.jobs.append((src, dst, r, min(step, rows - r)))
        self.i = 0

    def step(self, n):
        for _ in range(n):
            if self.i >= len(self.jobs):
                return
            src, dst, r, nn = self.jobs[self.i]
            d = self.nc.gpsimd.dma_start(out=self.io[dst][r:r + nn, :],
                                         in_=self.io[src][r:r + nn, :])
            self.kb.inc(d, "qG")
            self.i += 1

    def drain(self):
        self.step(len(self.jobs))
        return self.kb.t("qG")



def _qsync(nc, kb):
    kb.w(nc.sync, kb.t("qG"), kb.t("qA"))
    kb.w(nc.gpsimd, kb.t("qS"), kb.t("qA"))
    kb.w(nc.scalar, kb.t("qS"), kb.t("qG"))


def emit_body(nc, kb, io):
    import contextlib
    with contextlib.ExitStack() as es:
        S = lambda n, sh, dt: es.enter_context(nc.sbuf_tensor(n, sh, dt))
        m2t = S("m2t_sb", [128, 128], F32)
        ones3n = S("o3n_sb", [128, 3], F32)
        idb = S("idb_sb", [128, 128], BF16)
        id8 = S("id8_sb", [8, 8], F32)
        trimask = S("tri_sb", [128, 512], F32)
        onesrow = S("or_sb", [1, 128], F32)
        onescol1 = S("oc_sb", [128, 1], F32)
        normw = S("nw_sb", [128, 1], F32)
        dynsb = S("dyn_sb", [1, 4], I32)
        constv = S("constv_sb", [128, 3], F32)
        consts = {"m2t": m2t, "ones3n": ones3n, "idb": idb, "id8": id8,
                  "trimask": trimask, "onesrow": onesrow,
                  "onescol1": onescol1, "normw": normw}
        for k, sbt in (("m2t", m2t), ("ones3n", ones3n), ("idb", idb),
                       ("id8", id8), ("trimask", trimask),
                       ("onesrow", onesrow), ("onescol1", onescol1),
                       ("normw", normw), ("dynidx", dynsb),
                       ("constv", constv)):
            tconst = kb.dma(nc.sync, "qS", out=sbt[:], in_=io[k])
        for vi, val in enumerate([1e-6, 1e-5, -0.5]):
            nc.const_aps.register(F32, val, constv[:, vi:vi + 1])
        for eng in (nc.scalar, nc.vector, nc.tensor, nc.gpsimd):
            kb.w(eng, tconst)
        rg = es.enter_context(nc.sync.register("dynreg"))
        kb.w(nc.sync, tconst)
        nc.sync.reg_load(rg, dynsb[0:1, 0:1])
        v_p3 = nc.sync.snap(rg, min_val=0, max_val=2048)
        nc.sync.reg_load(rg, dynsb[0:1, 1:2])
        v_og = nc.sync.snap(rg, min_val=0, max_val=1024)
        nc.sync.reg_load(rg, dynsb[0:1, 2:3])
        v_zr = nc.sync.snap(rg, min_val=0, max_val=1024)
        dyn = {"p3": v_p3, "og": v_og, "zr": v_zr}
        with nc.sbuf_tensor("zt", [128, 2048], BF16) as zt, \
             nc.sbuf_tensor("zf", [1, 512], F32) as zf:
            nc.vector.memset(zf[:], 0.0)
            zi = nc.vector.memset(zt[:], 0.0)
            tz = kb.inc(zi, "eV")
            kb.w(nc.sync, tz)
            kb.dma(nc.sync, "qS", out=io["s_cbi"], in_=zf[:])
            for r in range(16):
                kb.dma(nc.sync, "qS",
                       out=io["s_contrib"][r * 128:(r + 1) * 128, 0:2048],
                       in_=zt[:])
                kb.dma(nc.sync, "qS",
                       out=io["s_contrib"][r * 128:(r + 1) * 128, 2048:4096],
                       in_=zt[:])
        if "dev_ct0" in io:
            kb.w(nc.gpsimd, kb.t("qS"))
            kb.dma(nc.gpsimd, "qG", out=io["dev_ct0"], in_=io["s_contrib"])
            kb.w(nc.gpsimd, kb.t("qG"))
            for eng in (nc.sync, nc.scalar, nc.vector, nc.tensor):
                kb.w(eng, kb.t("qG"))
        tpre = {}
        for src, dst in (("wq", "s_bwq"), ("wk", "s_bwk"), ("xt", "s_xb"),
                         ("wv", "s_bwv"), ("ww", "s_bww"),
                         ("wf1", "s_bwf1"), ("wg1", "s_bwg1"),
                         ("wbs", "s_bwbs"), ("wf2s", "s_bwf2"),
                         ("wg2s", "s_bwg2")):
            rows = io[src].shape[0]
            step = 1024
            for r in range(0, rows, step):
                t_ = kb.dma(nc.gpsimd, "qG",
                            out=io[dst][r:r + min(step, rows - r), :],
                            in_=io[src][r:r + min(step, rows - r), :])
            tpre[dst] = t_
        wcast = WCast(nc, kb, io)
        if _DEV["stop"] in ("p1", "p2"):
            wcast.jobs = []
        for pass_id in range(2):
            emit_p1_pass(nc, kb, io, consts, pass_id, wcast, tpre)
            nc.all_engine_barrier()
            _qsync(nc, kb)
        if _DEV["stop"] in ("p1", "p2"):
            t_wcast = wcast.drain()
            if _DEV["stop"] == "p1":
                kb.w(nc.sync, kb.t("qS"))
                kb.w(nc.gpsimd, kb.t("qG"))
                return
            nc.all_engine_barrier()
            _qsync(nc, kb)
            if "dev_ct1" in io:
                kb.w(nc.gpsimd, kb.t("qS"))
                kb.dma(nc.gpsimd, "qG", out=io["dev_ct1"],
                       in_=io["s_contrib"])
                kb.w(nc.gpsimd, kb.t("qG"))
                for eng in (nc.sync, nc.scalar, nc.vector, nc.tensor):
                    kb.w(eng, kb.t("qG"))
            emit_p2(nc, kb, io, consts, dyn)
            kb.w(nc.sync, kb.t("qS"))
            if "dev_ct2" in io:
                kb.w(nc.gpsimd, kb.t("qS"))
                kb.dma(nc.gpsimd, "qG", out=io["dev_ct2"],
                       in_=io["s_contrib"])
                kb.w(nc.gpsimd, kb.t("qG"))
                kb.w(nc.sync, kb.t("qG"))
            kb.dma(nc.sync, "qS",
                   out=io["s_contrib"][DS(dyn["og"], 1024), :],
                   in_=io["s_og"][:, :])
            if "dev_ct3" in io:
                kb.w(nc.gpsimd, kb.t("qS"))
                kb.dma(nc.gpsimd, "qG", out=io["dev_ct3"],
                       in_=io["s_contrib"])
                kb.w(nc.gpsimd, kb.t("qG"))
                kb.w(nc.sync, kb.t("qG"))
            kb.w(nc.gpsimd, kb.t("qS"), kb.t("qG"))
            bi = nc.gpsimd.collective_compute(
                "AllReduce", ALU.add,
                replica_groups=[[0, 1], [2, 3], [4, 5], [6, 7]],
                ins=[io["s_cbi"]], outs=[io["s_cbo"]])
            kb.w(nc.gpsimd, kb.inc(bi, "cC"))
            if "dev_ct" in io:
                kb.dma(nc.gpsimd, "qG", out=io["dev_ct"],
                       in_=io["s_contrib"])
                kb.w(nc.gpsimd, kb.t("qG"))
            if _DEV.get("skip_cc"):
                kb.w(nc.sync, kb.t("qS"))
                return
            cc_i = nc.gpsimd.collective_compute(
                "AllReduce", ALU.add,
                replica_groups=[[0, 1], [2, 3], [4, 5], [6, 7]],
                ins=[io["s_contrib"]], outs=[io["s_ored"]])
            t_ar = kb.inc(cc_i, "cC")
            kb.w(nc.gpsimd, t_ar)
            if "dev_ored" in io:
                kb.dma(nc.gpsimd, "qG", out=io["dev_ored"],
                       in_=io["s_ored"])
                kb.w(nc.gpsimd, kb.t("qG"))
            kb.w(nc.sync, kb.t("qS"))
            return
        with nc.sbuf_tensor("wol0", [128, HID], F32) as w0, \
             nc.sbuf_tensor("wol1", [128, HID], F32) as w1, \
             nc.sbuf_tensor("wob0", [128, HID], BF16) as b0, \
             nc.sbuf_tensor("wob1", [128, HID], BF16) as b1:
            wol, wob = [w0, w1], [b0, b1]
            tload = [("qS", 0), ("qS", 0)]
            tuse = [("eV", 0), ("eV", 0)]
            tst = [("qS", 0), ("qS", 0)]
            for fc in range(FC):
                k = fc % 2
                kb.w(nc.sync, tuse[k])
                d = nc.sync.dma_start(
                    out=wol[k][:], in_=io["wo"][fc * 128:(fc + 1) * 128, :])
                tload[k] = kb.inc(d, "qS")
                kb.w(nc.vector, tload[k], tst[k])
                sc = nc.vector.tensor_scalar(
                    wob[k][:], wol[k][:], normw[:, 0:1], None, ALU.mult)
                tuse[k] = kb.inc(sc, "eV")
                kb.w(nc.sync, tuse[k])
                d = nc.sync.dma_start(
                    out=io["s_bwo"][fc * 128:(fc + 1) * 128, :],
                    in_=wob[k][:])
                tst[k] = kb.inc(d, "qS")
        t_wcast = wcast.drain()

        if _DEV["stop"] == "p1":
            kb.w(nc.sync, kb.t("qS"))
            kb.w(nc.gpsimd, kb.t("qG"))
            return
        nc.all_engine_barrier()
        _qsync(nc, kb)

        emit_p2(nc, kb, io, consts, dyn)
        kb.w(nc.sync, kb.t("qS"))
        kb.dma(nc.sync, "qS", out=io["s_contrib"][DS(dyn["og"], 1024), :],
               in_=io["s_og"][:, :])
        kb.w(nc.gpsimd, kb.t("qS"), kb.t("qG"))
        bi = nc.gpsimd.collective_compute(
            "AllReduce", ALU.add,
            replica_groups=[[0, 1], [2, 3], [4, 5], [6, 7]],
            ins=[io["s_cbi"]], outs=[io["s_cbo"]])
        kb.w(nc.gpsimd, kb.inc(bi, "cC"))
        if "dev_ct" in io:
            kb.dma(nc.gpsimd, "qG", out=io["dev_ct"], in_=io["s_contrib"])
            kb.w(nc.gpsimd, kb.t("qG"))
        cc_i = nc.gpsimd.collective_compute(
            "AllReduce", ALU.add,
            replica_groups=[[0, 1], [2, 3], [4, 5], [6, 7]],
            ins=[io["s_contrib"]], outs=[io["s_ored"]])
        t_ar = kb.inc(cc_i, "cC")

        if _DEV["stop"] == "p2":
            kb.w(nc.gpsimd, t_ar)
            if "dev_ored" in io:
                kb.dma(nc.gpsimd, "qG", out=io["dev_ored"],
                       in_=io["s_ored"])
                kb.w(nc.gpsimd, kb.t("qG"))
            kb.w(nc.sync, kb.t("qS"))
            return
        nc.all_engine_barrier()
        _qsync(nc, kb)

        t_wall = t_wcast
        kb.w(nc.scalar, kb.t("qS"))
        emit_p3(nc, kb, io, consts, dyn, t_ar, t_wall)
        kb.w(nc.sync, kb.t("qS"))


def build_nc():
    nc = bass.Bass("TRN2", target_bir_lowering=False, debug=False)
    io = {}
    io["xt"] = _dram(nc, "xt", [HID, T], F32, "ExternalInput")
    for nm in ("wq", "wk", "wv", "ww"):
        io[nm] = _dram(nc, nm, [HID, 1024], F32, "ExternalInput")
    for nm in ("cq", "ck", "cvs"):
        io[nm] = _dram(nc, nm, [1024, 4], F32, "ExternalInput")
    io["wf1"] = _dram(nc, "wf1", [HID, 128], F32, "ExternalInput")
    io["wg1"] = _dram(nc, "wg1", [HID, 128], F32, "ExternalInput")
    io["wf2s"] = _dram(nc, "wf2s", [128, 1024], F32, "ExternalInput")
    io["wg2s"] = _dram(nc, "wg2s", [128, 1024], F32, "ExternalInput")
    io["wbs"] = _dram(nc, "wbs", [HID, 8], F32, "ExternalInput")
    io["bg2s"] = _dram(nc, "bg2s", [1, 1024], F32, "ExternalInput")
    io["normw"] = _dram(nc, "normw", [128, 1], F32, "ExternalInput")
    io["wo"] = _dram(nc, "wo", [HID, HID], F32, "ExternalInput")
    io["shgT"] = _dram(nc, "shgT", [HID, 8192], F32, "ExternalInput")
    io["shuT"] = _dram(nc, "shuT", [HID, 8192], F32, "ExternalInput")
    io["shdT"] = _dram(nc, "shdT", [8192, HID], F32, "ExternalInput")
    io["rgT"] = _dram(nc, "rgT", [HID, 1024], F32, "ExternalInput")
    io["ruT"] = _dram(nc, "ruT", [HID, 1024], F32, "ExternalInput")
    io["rdT"] = _dram(nc, "rdT", [1024, HID], F32, "ExternalInput")
    io["m2t"] = _dram(nc, "m2t", [128, 128], F32, "ExternalInput")
    io["ones3n"] = _dram(nc, "ones3n", [128, 3], F32, "ExternalInput")
    io["idb"] = _dram(nc, "idb", [128, 128], BF16, "ExternalInput")
    io["id8"] = _dram(nc, "id8", [8, 8], F32, "ExternalInput")
    io["trimask"] = _dram(nc, "trimask", [128, 512], F32, "ExternalInput")
    io["onesrow"] = _dram(nc, "onesrow", [1, 128], F32, "ExternalInput")
    io["onescol1"] = _dram(nc, "onescol1", [128, 1], F32, "ExternalInput")
    io["dynidx"] = _dram(nc, "dynidx", [1, 4], I32, "ExternalInput")
    io["constv"] = _dram(nc, "constv", [128, 3], F32, "ExternalInput")
    for nm in ("s_qf", "s_kf", "s_vf", "s_wf"):
        io[nm] = _dram(nc, nm, [1024, T], BF16)
    io["s_xb"] = _dram(nc, "s_xb", [HID, T], BF16)
    for nm in ("s_bwq", "s_bwk", "s_bwv", "s_bww"):
        io[nm] = _dram(nc, nm, [HID, 1024], BF16)
    io["s_bwf1"] = _dram(nc, "s_bwf1", [HID, 128], BF16)
    io["s_bwg1"] = _dram(nc, "s_bwg1", [HID, 128], BF16)
    io["s_bwbs"] = _dram(nc, "s_bwbs", [HID, 8], BF16)
    io["s_bwf2"] = _dram(nc, "s_bwf2", [128, 1024], BF16)
    io["s_bwg2"] = _dram(nc, "s_bwg2", [128, 1024], BF16)
    io["s_cgrel"] = _dram(nc, "s_cgrel", [T, 1024], F32)
    io["s_beta"] = _dram(nc, "s_beta", [T, 8], F32)
    io["s_gate"] = _dram(nc, "s_gate", [T, 1024], BF16)
    io["s_dcol"] = _dram(nc, "s_dcol", [NCH, 128, 24], F32)
    io["s_og"] = _dram(nc, "s_og", [1024, T], BF16)
    io["s_cbi"] = _dram(nc, "s_cbi", [1, 512], F32)
    io["s_cbo"] = _dram(nc, "s_cbo", [1, 512], F32)
    io["s_contrib"] = _dram(nc, "s_contrib", [2048, T], BF16)
    io["s_ored"] = _dram(nc, "s_ored", [2048, T], BF16)
    io["s_bshg"] = _dram(nc, "s_bshg", [HID, 8192], BF16)
    io["s_bshu"] = _dram(nc, "s_bshu", [HID, 8192], BF16)
    io["s_bshd"] = _dram(nc, "s_bshd", [8192, HID], BF16)
    io["s_brg"] = _dram(nc, "s_brg", [HID, 1024], BF16)
    io["s_bru"] = _dram(nc, "s_bru", [HID, 1024], BF16)
    io["s_brd"] = _dram(nc, "s_brd", [1024, HID], BF16)
    io["s_bwo"] = _dram(nc, "s_bwo", [HID, HID], BF16)
    io["s_r2"] = _dram(nc, "s_r2", [HID, 2048], F32)
    io["outf"] = _dram(nc, "outf", [HID, 2048], F32, "ExternalOutput")
    if _DEV.get("p2dump") is not None:
        io["p2dbg"] = nc.dram_tensor("p2dbg", [20, 128, 1024], F32,
                                     kind="ExternalOutput").ap()
        io["p2dbgB"] = nc.dram_tensor("p2dbgB", [20, 128, 1024], BF16,
                                      kind="ExternalOutput").ap()
    if "dev_ored" in _DEV.get("taps", ()):
        io["dev_ored"] = nc.dram_tensor("dev_ored", [2048, T], BF16,
                                        kind="ExternalOutput").ap()
    if "dev_ct" in _DEV.get("taps", ()):
        io["dev_ct"] = nc.dram_tensor("dev_ct", [2048, T], BF16,
                                      kind="ExternalOutput").ap()
    if "dev_ct0" in _DEV.get("taps", ()):
        io["dev_ct0"] = nc.dram_tensor("dev_ct0", [2048, T], BF16,
                                       kind="ExternalOutput").ap()
    if "dev_ct1" in _DEV.get("taps", ()):
        io["dev_ct1"] = nc.dram_tensor("dev_ct1", [2048, T], BF16,
                                       kind="ExternalOutput").ap()
    for nm in ("dev_ct2", "dev_ct3"):
        if nm in _DEV.get("taps", ()):
            io[nm] = nc.dram_tensor(nm, [2048, T], BF16,
                                    kind="ExternalOutput").ap()

    kb = KB(nc)
    kb.sem("qS", "qA", "qG", "cC", "eP", "eV", "eA", "eG")
    with nc.cleanup_on_exit():
        emit_body(nc, kb, io)
        nc.all_engine_barrier()
    return nc



def _prep_in_maps(ins):
    x = np.ascontiguousarray(ins["L0_attn_hidden_states"], np.float32)
    B = x.shape[0]
    xts = [np.ascontiguousarray(x[b].T) for b in range(B)]
    Wq, Wk, Wv, Ww = (ins[f"L0_attn_W{c}"] for c in "qkvw")
    cq, ck, cv = ins["L0_attn_cq"], ins["L0_attn_ck"], ins["L0_attn_cv"]
    shgT = np.ascontiguousarray(ins["L0_ffn_sh_gate"].T)
    shuT = np.ascontiguousarray(ins["L0_ffn_sh_up"].T)
    shdT = np.ascontiguousarray(ins["L0_ffn_sh_down"].T)
    rgT = np.ascontiguousarray(ins["L0_ffn_r_gate"].T)
    ruT = np.ascontiguousarray(ins["L0_ffn_r_up"].T)
    rdT = np.ascontiguousarray(ins["L0_ffn_r_down"].T)
    tau = np.arange(128)[:, None]
    tt = np.arange(128)[None, :]
    m2t = ((tau <= MID).astype(np.float32) - (tau <= tt).astype(np.float32))
    ones3n = np.zeros((128, 3), np.float32)
    ones3n[:, 0] = -1.0
    ones3n[64:, 1] = -1.0
    ones3n[:64, 2] = -1.0
    idb = np.eye(128, dtype=NPBF)
    id8 = np.eye(8, dtype=np.float32)
    tri = (tau <= tt).astype(np.float32)
    trimask = np.tile(tri, (1, 4)).astype(np.float32)
    onesrow = np.ones((1, 128), np.float32)
    onescol1 = np.ones((128, 1), np.float32)
    in_maps = []
    for c in range(8):
        b, hg = c // 2, c % 2
        hf = hg
        sl = slice(hg * 1024, (hg + 1) * 1024)
        m = {
            "xt": xts[b],
            "wq": np.ascontiguousarray(Wq[:, sl]),
            "wk": np.ascontiguousarray(Wk[:, sl]),
            "wv": np.ascontiguousarray(Wv[:, sl]),
            "ww": np.ascontiguousarray(Ww[:, sl]),
            "cq": np.ascontiguousarray(cq[sl]),
            "ck": np.ascontiguousarray(ck[sl]),
            "cvs": np.ascontiguousarray(cv[sl]),
            "wf1": ins["L0_attn_Wf1"], "wg1": ins["L0_attn_Wg1"],
            "wf2s": np.ascontiguousarray(ins["L0_attn_Wf2"][:, sl]),
            "wg2s": np.ascontiguousarray(ins["L0_attn_Wg2"][:, sl]),
            "wbs": np.ascontiguousarray(
                ins["L0_attn_Wb"][:, hg * 8:(hg + 1) * 8]),
            "bg2s": np.ascontiguousarray(ins["L0_attn_bg2"][sl][None, :]),
            "normw": np.ascontiguousarray(
                ins["L0_attn_norm_w"][:, None], np.float32),
            "wo": ins["L0_attn_Wo"],
            "shgT": shgT, "shuT": shuT, "shdT": shdT,
            "rgT": rgT, "ruT": ruT, "rdT": rdT,
            "m2t": m2t, "ones3n": ones3n, "idb": idb, "id8": id8,
            "trimask": trimask, "onesrow": onesrow, "onescol1": onescol1,
            "dynidx": np.array(
                [[hf * 2048, hg * 1024, (1 - hg) * 1024, 0]], np.int32),
            "constv": np.tile(np.array(
                [[1e-6, 1e-5, -0.5]], np.float32), (128, 1)),
        }
        in_maps.append(m)
    return in_maps


def kernel(**inputs):
    ins = {k: np.ascontiguousarray(np.asarray(v)) for k, v in inputs.items()}
    assert ins["L0_ffn_router_weight"].shape == (8, HID)
    in_maps = _prep_in_maps(ins)
    nc = build_nc()
    res = run_bass_kernel_spmd(nc, in_maps, core_ids=list(range(8)))
    out = np.empty((4, T, HID), np.float32)
    for c in range(8):
        b, hf = c // 2, c % 2
        out[b, hf * 2048:(hf + 1) * 2048, :] = res.results[c]["outf"].T
    return out


def _dev_run(in_maps, taps=(), stop=None):
    _DEV["taps"] = tuple(taps)
    _DEV["stop"] = stop
    try:
        nc = build_nc()
        return run_bass_kernel_spmd(nc, in_maps, core_ids=list(range(8)))
    finally:
        _DEV["taps"] = ()
        _DEV["stop"] = None
